# version 01
# baseline (speedup 1.0000x reference)
import contextlib
import numpy as np
import ml_dtypes
import concourse.bass as bass
import concourse.mybir as mybir
from concourse.bass_utils import run_bass_kernel_spmd

F32 = mybir.dt.float32
BF16 = mybir.dt.bfloat16
I32 = mybir.dt.int32
U32 = mybir.dt.uint32
ALU = mybir.AluOpType
AF = mybir.ActivationFunctionType
AX = mybir.AxisListType

NCORES = 8
B, S, H = 16, 2048, 1024
BPC = B // NCORES
T = BPC * S
TT = B * S
E, HF = 8, 4096
CAP = TT // E
NHB = H // 128
NT = T // 128
TPB = NT // BPC
CPB = 4
EPS = 1e-5
BIGF = 1.0e9
SENT = 0x7FFFFFF0


def _bf(a):
    return np.ascontiguousarray(a).astype(ml_dtypes.bfloat16)


def build_l1():
    nc = bass.Bass("TRN2", target_bir_lowering=False, debug=False,
                   num_devices=NCORES)
    x_in = nc.dram_tensor("x_in", [BPC, S, H], F32, kind="ExternalInput").ap()
    td_in = nc.dram_tensor("td_in", [H], F32, kind="ExternalInput").ap()
    tmv_in = nc.dram_tensor("tmv_in", [H], F32, kind="ExternalInput").ap()
    tmr_in = nc.dram_tensor("tmr_in", [H], F32, kind="ExternalInput").ap()
    wv_in = nc.dram_tensor("wv_in", [H, H], F32, kind="ExternalInput").ap()
    wr_in = nc.dram_tensor("wr_in", [H, H], F32, kind="ExternalInput").ap()
    wo_in = nc.dram_tensor("wo_in", [H, H], F32, kind="ExternalInput").ap()
    wg_in = nc.dram_tensor("wg_in", [H, E], F32, kind="ExternalInput").ap()
    id_in = nc.dram_tensor("id_in", [8, 8], F32, kind="ExternalInput").ap()

    xn_out = nc.dram_tensor("xn_out", [T, H], BF16, kind="ExternalOutput").ap()
    attn_out = nc.dram_tensor("attn_out", [T, H], BF16, kind="ExternalOutput").ap()
    u_out = nc.dram_tensor("u_out", [128, NT, E], F32, kind="ExternalOutput").ap()
    gvr_out = nc.dram_tensor("gvr_out", [128, NT], F32, kind="ExternalOutput").ap()

    ctx = contextlib.ExitStack()
    sbt = lambda n, s, d=F32: ctx.enter_context(nc.sbuf_tensor(n, s, d))
    pst = lambda n, s, d=F32: ctx.enter_context(nc.psum_tensor(n, s, d))
    sem = lambda n: ctx.enter_context(nc.semaphore(n))

    dcol = sbt("dcol", [128, NHB])
    tmvc = sbt("tmvc", [128, NHB])
    tmrc = sbt("tmrc", [128, NHB])
    tdc = sbt("tdc", [128, NHB])
    wgh = sbt("wgh", [128, NHB, E], BF16)
    wgl = sbt("wgl", [128, NHB, E], BF16)
    wgf = sbt("wgf", [128, NHB, E])
    ident8 = sbt("ident8", [8, 8])
    wblk = [sbt(f"wblk{k}", [128, H], BF16) for k in range(2)]
    wT = {nm: sbt(f"w{nm}T", [128, NHB, NHB, 128], BF16) for nm in "vro"}

    NXB = 2
    xt = [sbt(f"xt{k}", [128, H]) for k in range(NXB)]
    xdh = [sbt(f"xdh{k}", [128, H], BF16) for k in range(NXB)]
    xdl = [sbt(f"xdl{k}", [128, H], BF16) for k in range(NXB)]
    xnt = [sbt(f"xnt{k}", [128, H], BF16) for k in range(NXB)]
    mcol = sbt("mcol", [128, NT])
    invc = sbt("invc", [128, NT])
    epsc = sbt("epsc", [128, 1])

    xn_fm = sbt("xn_fm", [128, NHB, S], BF16)
    xdhC = [sbt(f"xdhC{k}", [128, NHB, 512], BF16) for k in range(2)]
    xdlC = [sbt(f"xdlC{k}", [128, NHB, 512], BF16) for k in range(2)]
    zbuf = sbt("zbuf", [128, NHB, S + 1], BF16)
    vinC = sbt("vinC", [128, NHB, 512], BF16)
    rinC = sbt("rinC", [128, NHB, 512], BF16)
    scrA = sbt("scrA", [128, 512])
    dif = scrA[:].bitcast(BF16)[:, 0:512]
    usb = scrA[0:8, :]

    vsb = sbt("vsb", [128, NHB, 512], BF16)
    rsb = sbt("rsb", [128, NHB, 512], BF16)
    asb = sbt("asb", [128, 512], BF16)
    atm = sbt("atm", [128, 2, 4, 128], BF16)

    ufat = sbt("ufat", [128, NT, E])
    gvr = sbt("gvr", [128, NT])
    g1 = scrA[:].rearrange("p (f e) -> p f e", e=8)[:, 0:NT, :]
    g2 = scrA[:, 480:480 + NT]

    ps_v = pst("ps_v", [128, 512])
    ps_r = pst("ps_r", [128, 512])
    ps_o = pst("ps_o", [128, 512])
    ps_u = pst("ps_u", [8, 512])
    ps_t = pst("ps_t", [128, 32])

    sems = {n: sem(n) for n in
            ["s_prm", "s_w", "s_xl", "s_vec", "s_scl", "s_xb", "s_mm",
             "s_st", "s_gp"]}
    cnt = {n: 0 for n in sems}

    OPS = {k: [] for k in ["sync", "vector", "scalar", "tensor", "gpsimd"]}

    def on(eng, fn):
        OPS[eng].append(fn)

    def inc(sname, k=1):
        cnt[sname] += k
        return cnt[sname]

    def wait(eng, sname, val):
        if val <= 0:
            return
        on(eng, lambda e, s=sems[sname], v=val: e.wait_ge(s, v))

    def op(eng, sname, emit, k=1):
        v = inc(sname, k)
        on(eng, lambda e, s=sems[sname], kk=k, em=emit: em(e).then_inc(s, kk))
        return v

    prm_v = 0
    for dst, srcp in ((tdc, td_in), (tmvc, tmv_in), (tmrc, tmr_in)):
        prm_v = op("sync", "s_prm",
                   lambda e, d=dst, s=srcp: e.dma_start(
                       out=d[:], in_=s.rearrange("(hb p) -> p hb", p=128),
                       allow_slow_non_contiguous=True), 16)
    prm_v = op("sync", "s_prm",
               lambda e: e.dma_start(
                   out=wgf[:], in_=wg_in.rearrange("(hb p) e -> p hb e", p=128),
                   allow_slow_non_contiguous=True), 16)
    prm_v = op("sync", "s_prm",
               lambda e: e.dma_start(out=ident8[:], in_=id_in), 16)

    wait("scalar", "s_prm", prm_v)
    d1_v = op("scalar", "s_scl", lambda e: e.activation(dcol[:], tdc[:], AF.Exp))
    wait("scalar", "s_scl", d1_v)
    dready = op("scalar", "s_scl",
                lambda e: e.activation(dcol[:], dcol[:], AF.Exp, scale=-1.0))
    wait("vector", "s_prm", prm_v)
    epv = op("vector", "s_vec", lambda e: e.memset(epsc[:], EPS))
    op("vector", "s_vec", lambda e: e.tensor_copy(wgh[:], wgf[:]))
    wg_v = op("vector", "s_vec", lambda e: e.tensor_sub(wgl[:], wgf[:], wgh[:]))

    wx_done = {}
    wld = {}
    nblk = 0
    for nm, srcp in (("v", wv_in), ("r", wr_in), ("o", wo_in)):
        for rb in range(NHB):
            k2 = nblk % 2
            if nblk >= 2:
                wait("gpsimd", "s_xb", wx_done[nblk - 2])
            wld[nblk] = op("gpsimd", "s_w",
                           lambda e, s=srcp, r=rb, k_=k2: e.dma_start(
                               out=wblk[k_][:],
                               in_=s[r * 128:(r + 1) * 128, :]
                               .rearrange("p c -> p c")), 16)
            wait("scalar", "s_w", wld[nblk])
            wx_done[nblk] = op("scalar", "s_xb",
                               lambda e, k_=k2, r=rb, t=wT[nm]: e.dma_start(
                                   out=t[:, :, r, :], in_=wblk[k_][:],
                                   transpose=True), 16)
            nblk += 1
    wxv = {"v": wx_done[7], "r": wx_done[15], "o": wx_done[23]}

    xb_tile = {}
    xn_store = {}
    x_free = {}
    u_ev = {}
    log_mm = {}

    for b in range(BPC):
        for i in range(TPB):
            g = b * TPB + i
            k = g % NXB
            cg = b * CPB + i // 4
            if g >= NXB:
                gp = g - NXB
                wait("sync", "s_vec", x_free[gp])
                wait("sync", "s_xb", xb_tile[gp])
                wait("sync", "s_st", xn_store[gp])
            xl_v = op("sync", "s_xl",
                      lambda e, b_=b, i_=i, k_=k: e.dma_start(
                          out=xt[k_][:], in_=x_in[b_, i_ * 128:(i_ + 1) * 128, :]), 16)
            wait("vector", "s_xl", xl_v)
            rs_v = op("vector", "s_vec",
               lambda e, k_=k, g_=g: e.reduce_sum(mcol[:, g_:g_ + 1], xt[k_][:],
                                                  axis=AX.X))
            wait("vector", "s_vec", rs_v)
            mv = op("vector", "s_vec", lambda e, g_=g: e.tensor_scalar_mul(
                mcol[:, g_:g_ + 1], mcol[:, g_:g_ + 1], -1.0 / H))
            wait("vector", "s_vec", mv)
            wait("scalar", "s_vec", mv)
            op("scalar", "s_scl", lambda e, k_=k, g_=g: e.activation(
                xdh[k_][:], xt[k_][:], AF.Identity, bias=mcol[:, g_:g_ + 1]))
            sqa_v = op("scalar", "s_scl", lambda e, k_=k, g_=g: e.activation(
                xdl[k_][:], xt[k_][:], AF.Square, bias=mcol[:, g_:g_ + 1],
                accum_out=invc[:, g_:g_ + 1]))
            wait("scalar", "s_scl", sqa_v)
            iv_v = op("scalar", "s_scl", lambda e, g_=g: e.activation(
                invc[:, g_:g_ + 1], invc[:, g_:g_ + 1], AF.Ln, scale=1.0 / H,
                bias=epsc[:, 0:1]))
            wait("scalar", "s_scl", iv_v)
            iv_v = op("scalar", "s_scl", lambda e, g_=g: e.activation(
                invc[:, g_:g_ + 1], invc[:, g_:g_ + 1], AF.Exp, scale=-0.5))
            wait("vector", "s_scl", iv_v)
            op("vector", "s_vec", lambda e, k_=k, g_=g: e.scalar_tensor_tensor(
                xdl[k_][:], xt[k_][:], mcol[:, g_:g_ + 1], xdh[k_][:],
                ALU.add, ALU.subtract))
            xn_v = op("vector", "s_vec", lambda e, k_=k, g_=g: e.tensor_scalar(
                xnt[k_][:], xt[k_][:], mcol[:, g_:g_ + 1], invc[:, g_:g_ + 1],
                ALU.add, ALU.mult))
            wait("vector", "s_vec", xn_v)
            x_free[g] = xn_v
            wait("gpsimd", "s_vec", xn_v)
            xn_store[g] = op("gpsimd", "s_st", lambda e, k_=k, g_=g: e.dma_start(
                out=xn_out[g_ * 128:(g_ + 1) * 128, :], in_=xnt[k_][:]), 16)
            wait("scalar", "s_vec", xn_v)
            if cg >= 2 and i % 4 == 0:
                wait("scalar", "s_mm", log_mm[cg - 2])
            op("scalar", "s_xb",
               lambda e, k_=k, i_=i: e.dma_start(
                   out=xn_fm[:, :, i_ * 128:(i_ + 1) * 128],
                   in_=xnt[k_][:], transpose=True), 16)
            q = (i % 4) * 128
            op("scalar", "s_xb",
               lambda e, k_=k, c_=cg % 2, q_=q: e.dma_start(
                   out=xdhC[c_][:, :, q_:q_ + 128], in_=xdh[k_][:],
                   transpose=True), 16)
            xb_tile[g] = op("scalar", "s_xb",
                            lambda e, k_=k, c_=cg % 2, q_=q: e.dma_start(
                                out=xdlC[c_][:, :, q_:q_ + 128], in_=xdl[k_][:],
                                transpose=True), 16)

            if i % 4 == 3:
                cw = i // 4
                wait("tensor", "s_xb", xb_tile[g])
                if cg > 0:
                    wait("tensor", "s_vec", u_ev[cg - 1][0])
                    wait("tensor", "s_vec", u_ev[cg - 1][1])
                else:
                    wait("tensor", "s_vec", wg_v)
                    wait("tensor", "s_prm", prm_v)
                first = True
                for lhs, rhs_fm in ((wgh, xdhC[cg % 2]), (wgl, xdhC[cg % 2]),
                                    (wgh, xdlC[cg % 2])):
                    for hb in range(NHB):
                        st_ = first
                        first = False
                        last = (rhs_fm is xdlC[cg % 2] and hb == NHB - 1)
                        op("tensor", "s_mm",
                           lambda e, l=lhs, r=rhs_fm, h=hb, s0=st_, s1=last:
                           e.matmul(ps_u[:], l[:, h, :], r[:, h, :],
                                    start=s0, stop=s1))
                log_mm[cg] = cnt["s_mm"]
                wait("vector", "s_mm", log_mm[cg])
                uev = op("vector", "s_vec", lambda e: e.tensor_copy(usb, ps_u[:]))
                wait("tensor", "s_vec", uev)
                for kq in range(4):
                    op("tensor", "s_mm", lambda e, k_=kq: e.transpose(
                        ps_t[:, k_ * 8:(k_ + 1) * 8],
                        usb[:, k_ * 128:(k_ + 1) * 128], ident8[:]))
                tr_v = cnt["s_mm"]
                wait("vector", "s_mm", tr_v)
                ue2 = op("vector", "s_vec", lambda e, c_=cg: e.tensor_copy(
                    ufat[:, c_ * 4:(c_ + 1) * 4, :],
                    ps_t[:].rearrange("p (k e) -> p k e", e=8)))
                u_ev[cg] = (uev, ue2)

        wait("vector", "s_xb", xb_tile[b * TPB + TPB - 1])
        if b == 0:
            wait("vector", "s_scl", dready)
        else:
            wait("vector", "s_mm", cnt["s_mm"])
        for hb in range(NHB):
            op("vector", "s_vec", lambda e, h=hb: e.memset(zbuf[:, h, 0:1], 0.0))
            sc_v = op("vector", "s_vec", lambda e, h=hb: e.tensor_tensor_scan(
                zbuf[:, h, 1:S + 1],
                dcol[:, h:h + 1].broadcast_to([128, S]),
                xn_fm[:, h, :], 0.0, ALU.mult, ALU.add))
            wait("vector", "s_vec", sc_v)

        for cw in range(CPB):
            tsl = slice(cw * 512, (cw + 1) * 512)
            zsl = slice(cw * 512, cw * 512 + 512)
            if b == 0 and cw == 0:
                wait("tensor", "s_xb", wxv["o"])
            if not (b == 0 and cw == 0):
                wait("vector", "s_mm", cnt["s_mm"])
            for hb in range(NHB):
                df_v = op("vector", "s_vec", lambda e, h=hb, t_=tsl, z_=zsl: e.tensor_sub(
                    dif, xn_fm[:, h, t_], zbuf[:, h, z_]))
                wait("vector", "s_vec", df_v)
                op("vector", "s_vec", lambda e, h=hb, z_=zsl: e.scalar_tensor_tensor(
                    vinC[:, h, :], dif, tmvc[:, h:h + 1], zbuf[:, h, z_],
                    ALU.mult, ALU.add))
                mx_v = op("vector", "s_vec",
                          lambda e, h=hb, z_=zsl: e.scalar_tensor_tensor(
                              rinC[:, h, :], dif, tmrc[:, h:h + 1],
                              zbuf[:, h, z_], ALU.mult, ALU.add))
                wait("vector", "s_vec", mx_v)
            mix_v = cnt["s_vec"]
            for nm, psum, outsb in (("v", ps_v, vsb), ("r", ps_r, rsb)):
                inb = vinC if nm == "v" else rinC
                for ob in range(NHB):
                    if ob == 0 and nm == "v":
                        wait("tensor", "s_vec", mix_v)
                    for hb in range(NHB):
                        op("tensor", "s_mm",
                           lambda e, n=nm, o_=ob, h=hb, ib=inb, p_=psum:
                           e.matmul(p_[:], wT[n][:, h, o_, :], ib[:, h, :],
                                    start=(h == 0), stop=(h == NHB - 1)))
                    mm_v = cnt["s_mm"]
                    if nm == "v":
                        wait("vector", "s_mm", mm_v)
                        ev_v = op("vector", "s_vec",
                                  lambda e, o_=ob, p_=psum: e.tensor_copy(
                                      vsb[:, o_, :], p_[:]))
                        wait("tensor", "s_vec", ev_v)
                    else:
                        wait("scalar", "s_mm", mm_v)
                        ev_v = op("scalar", "s_scl",
                                  lambda e, o_=ob, p_=psum: e.activation(
                                      rsb[:, o_, :], p_[:], AF.Sigmoid))
                        wait("tensor", "s_scl", ev_v)
            atm_st = {}
            for mb in range(NHB):
                for ob in range(NHB):
                    op("tensor", "s_mm",
                       lambda e, m_=mb, o_=ob: e.matmul(
                           ps_o[:], wT["o"][:, o_, m_, :], vsb[:, o_, :],
                           start=(o_ == 0), stop=(o_ == NHB - 1)))
                mm_v = cnt["s_mm"]
                wait("vector", "s_mm", mm_v)
                ga_v = op("vector", "s_vec", lambda e, m_=mb: e.tensor_mul(
                    asb[:], rsb[:, m_, :], ps_o[:]))
                wait("tensor", "s_vec", ga_v)
                wait("scalar", "s_vec", ga_v)
                if mb >= 2:
                    wait("scalar", "s_st", atm_st[(mb - 2) & ~1])
                xb_v = op("scalar", "s_xb", lambda e, m_=mb: e.dma_start(
                    out=atm[:, m_ % 2, :, :], in_=asb[:], transpose=True), 16)
                wait("vector", "s_xb", xb_v)
                wait("gpsimd", "s_xb", cnt["s_xb"])
                t0 = b * S + cw * 512
                atm_st[mb & ~1] = op("gpsimd", "s_st",
                    lambda e, t0_=t0, m_=mb: e.dma_start(
                        out=attn_out.rearrange("(q p) (mb r) -> q p mb r",
                                               p=128, r=128)
                            [t0_ // 128:t0_ // 128 + 4, :, m_, :]
                            .transpose([1, 0, 2]),
                        in_=atm[:, m_ % 2, :, :]), 16)

    wait("vector", "s_vec", u_ev[BPC * CPB - 1][1])
    t_v = op("vector", "s_vec", lambda e: e.reduce_max(g2, ufat[:], axis=AX.X))
    wait("vector", "s_vec", t_v)
    t_v = op("vector", "s_vec", lambda e: e.tensor_tensor(
        g1, ufat[:], g2.unsqueeze(2).broadcast_to([128, NT, E]),
        ALU.subtract))
    wait("vector", "s_vec", t_v)
    gm_v = op("vector", "s_vec", lambda e: e.tensor_tensor(
        g1, g1, invc[:].unsqueeze(2).broadcast_to([128, NT, E]),
        ALU.mult))
    wait("scalar", "s_vec", gm_v)
    ge_v = op("scalar", "s_scl", lambda e: e.activation(g1, g1, AF.Exp))
    wait("vector", "s_scl", ge_v)
    t_v = op("vector", "s_vec", lambda e: e.reduce_sum(g2, g1, axis=AX.X))
    wait("vector", "s_vec", t_v)
    gr_v = op("vector", "s_vec", lambda e: e.reciprocal(gvr[:], g2))
    wait("gpsimd", "s_vec", gr_v)
    op("gpsimd", "s_st", lambda e: e.dma_start(out=u_out, in_=ufat[:]), 16)
    op("gpsimd", "s_st", lambda e: e.dma_start(out=gvr_out, in_=gvr[:]), 16)

    wait("gpsimd", "s_st", cnt["s_st"])

    block = ctx.enter_context(nc.Block())

    @block.sync
    def _(e):
        for f in OPS["sync"]:
            f(e)

    @block.vector
    def _(e):
        for f in OPS["vector"]:
            f(e)

    @block.scalar
    def _(e):
        for f in OPS["scalar"]:
            f(e)

    @block.tensor
    def _(e):
        for f in OPS["tensor"]:
            f(e)

    @block.gpsimd
    def _(e):
        for f in OPS["gpsimd"]:
            f(e)

    ctx.close()
    return nc


def l1_inputs(full):
    x = np.ascontiguousarray(np.asarray(full["L0_attn_x"], np.float32))
    ident = np.eye(8, dtype=np.float32)
    maps = []
    for c in range(NCORES):
        maps.append(dict(
            x_in=x[c * BPC:(c + 1) * BPC],
            td_in=np.asarray(full["L0_attn_time_decay"], np.float32),
            tmv_in=np.asarray(full["L0_attn_time_mix_v"], np.float32),
            tmr_in=np.asarray(full["L0_attn_time_mix_r"], np.float32),
            wv_in=np.asarray(full["L0_attn_Wv"], np.float32),
            wr_in=np.asarray(full["L0_attn_Wr"], np.float32),
            wo_in=np.asarray(full["L0_attn_Wo"], np.float32),
            wg_in=np.asarray(full["L0_ffn_wg"], np.float32),
            id_in=ident,
        ))
    return maps


NF = NCORES * NT


def build_l2a():
    nc = bass.Bass("TRN2", target_bir_lowering=False, debug=False,
                   num_devices=NCORES)
    u_all = nc.dram_tensor("u_all", [128, NF, E], F32, kind="ExternalInput").ap()
    gvr_in = nc.dram_tensor("gvr_in", [128, NT], F32, kind="ExternalInput").ap()
    tri_in = nc.dram_tensor("tri_in", [128, 128], F32, kind="ExternalInput").ap()
    eidx_in = nc.dram_tensor("eidx_in", [128, E], F32, kind="ExternalInput").ap()
    ebig_in = nc.dram_tensor("ebig_in", [128, E], F32, kind="ExternalInput").ap()
    ones_in = nc.dram_tensor("ones_in", [128, 1], F32, kind="ExternalInput").ap()
    onesr_in = nc.dram_tensor("onesr_in", [1, 128], F32, kind="ExternalInput").ap()
    iotg_in = nc.dram_tensor("iotg_in", [128, NT], I32, kind="ExternalInput").ap()
    cid_in = nc.dram_tensor("cid_in", [1, 1], I32, kind="ExternalInput").ap()

    slotg_out = nc.dram_tensor("slotg_out", [128, NT], I32, kind="ExternalOutput").ap()
    gate_out = nc.dram_tensor("gate_out", [128, NT], F32, kind="ExternalOutput").ap()
    invp_out = nc.dram_tensor("invp_out", [TT, 1], I32, kind="ExternalOutput").ap()

    scr3 = nc.dram_tensor("scr3", [3, 128, NF], F32, kind="Internal").ap()

    ctx = contextlib.ExitStack()
    sbt = lambda n, s, d=F32: ctx.enter_context(nc.sbuf_tensor(n, s, d))
    pst = lambda n, s, d=F32: ctx.enter_context(nc.psum_tensor(n, s, d))
    sem = lambda n: ctx.enter_context(nc.semaphore(n))

    uf = sbt("uf", [128, NF, E])
    gvrs = sbt("gvrs", [128, NT])
    trif = sbt("trif", [128, 128])
    tri = sbt("tri", [128, 128], BF16)
    eidx = sbt("eidx", [128, E])
    ebig = sbt("ebig", [128, E])
    onesf = sbt("onesf", [128, 1])
    onesr = sbt("onesr", [1, 128])
    onesb = sbt("onesb", [128, 1], BF16)
    iotg = sbt("iotg", [128, NT], I32)
    cids = sbt("cids", [1, 1], I32)
    mfat = sbt("mfat", [128, NF])
    w1f = sbt("w1f", [128, NF, E])
    w2f = sbt("w2f", [128, NF])
    ohb = sbt("ohb", [128, NF, E], BF16)
    ohf = sbt("ohf", [128, NF, E])
    etf = sbt("etf", [128, NF])
    keepf = sbt("keepf", [128, NF])
    hsb = sbt("hsb", [1, NF * E])
    bases = sbt("bases", [1, (NF + 1) * E])
    onec = sbt("onec", [1, 1])
    e_own = sbt("e_own", [128, NT])
    p_own = sbt("p_own", [128, NT])
    k_own = sbt("k_own", [128, NT])
    gate = sbt("gate", [128, NT])
    sl1 = sbt("sl1", [128, NT])
    sl2 = sbt("sl2", [128, NT])
    slots = sbt("slots", [128, NT], U32)
    sentt = sbt("sentt", [128, NF], I32)

    ps_pr = pst("ps_pr", [128, NF, E])
    ps_h = pst("ps_h", [1, 512])

    sems = {n: sem(n) for n in ["s_ld", "s_vec", "s_mm", "s_st", "s_gp"]}
    cnt = {n: 0 for n in sems}
    OPS = {k: [] for k in ["sync", "vector", "scalar", "tensor", "gpsimd"]}

    def on(eng, fn):
        OPS[eng].append(fn)

    def inc(s, k=1):
        cnt[s] += k
        return cnt[s]

    def wait(eng, s, v):
        if v > 0:
            on(eng, lambda e, ss=sems[s], vv=v: e.wait_ge(ss, vv))

    def op(eng, s, emit, k=1):
        v = inc(s, k)
        on(eng, lambda e, ss=sems[s], kk=k, em=emit: em(e).then_inc(ss, kk))
        return v

    def vop(emit):
        v = op("vector", "s_vec", emit)
        wait("vector", "s_vec", v)
        return v

    ld_v = 0
    for dst, srcp in ((uf, u_all), (gvrs, gvr_in), (trif, tri_in), (eidx, eidx_in),
                      (ebig, ebig_in), (onesf, ones_in), (onesr, onesr_in), (iotg, iotg_in),
                      (cids, cid_in)):
        ld_v = op("sync", "s_ld", lambda e, d=dst, s=srcp: e.dma_start(
            out=d[:], in_=s), 16)

    wait("vector", "s_ld", ld_v)
    vop(lambda e: e.tensor_copy(tri[:], trif[:]))
    vop(lambda e: e.tensor_copy(onesb[:], onesf[:]))
    vop(lambda e: e.memset(onec[:], 1.0))
    vop(lambda e: e.memset(bases[:, 0:E], 0.0))
    vop(lambda e: e.memset(sentt[:], float(SENT)))
    vop(lambda e: e.reduce_max(mfat[:], uf[:], axis=AX.X))
    vop(lambda e: e.tensor_tensor(
        w1f[:], uf[:], mfat[:].unsqueeze(2).broadcast_to([128, NF, E]),
        ALU.is_equal))
    vop(lambda e: e.tensor_tensor(
        w1f[:], w1f[:], ebig[:].unsqueeze(1).broadcast_to([128, NF, E]), ALU.mult))
    vop(lambda e: e.tensor_reduce(w2f[:], w1f[:], AX.X, ALU.min))
    vop(lambda e: e.tensor_scalar_add(etf[:], w2f[:], 1.0e6))
    vop(lambda e: e.tensor_tensor(
        ohf[:], etf[:].unsqueeze(2).broadcast_to([128, NF, E]),
        eidx[:].unsqueeze(1).broadcast_to([128, NF, E]), ALU.is_equal))
    oh_v = vop(lambda e: e.tensor_copy(ohb[:], ohf[:]))

    wait("tensor", "s_vec", oh_v)
    for q in range(4):
        mmh = op("tensor", "s_mm", lambda e, q_=q: e.matmul(
            ps_h[:], onesb[:], ohb[:, q_ * 64:(q_ + 1) * 64, :],
            start=True, stop=True))
        wait("vector", "s_mm", mmh)
        hv = vop(lambda e, q_=q: e.tensor_copy(
            hsb[:, q_ * 512:(q_ + 1) * 512], ps_h[:]))
        wait("tensor", "s_vec", hv)

    bt = bases[0:1, :].rearrange("o (f e) -> o f e", e=E)
    ht = hsb[0:1, :].rearrange("o (f e) -> o f e", e=E)
    for ei in range(E):
        vop(lambda e, e_=ei: e.tensor_tensor_scan(
            bt[:, 1:NF + 1, e_], onec[:].broadcast_to([1, NF]),
            ht[:, :, e_], 0.0, ALU.mult, ALU.add))
    bs_v = cnt["s_vec"]

    wait("tensor", "s_vec", bs_v)
    for f in range(NF):
        op("tensor", "s_mm", lambda e, f_=f: e.matmul(
            ps_pr[:, f_, :], tri[:], ohb[:, f_, :], start=True, stop=False))
        op("tensor", "s_mm", lambda e, f_=f: e.matmul(
            ps_pr[:, f_, :], onesr[:],
            bt[:, f_, :], start=False, stop=True))
    pr_v = cnt["s_mm"]

    wait("vector", "s_mm", pr_v)
    vop(lambda e: e.tensor_tensor(w1f[:], ohf[:], ps_pr[:], ALU.mult))
    vop(lambda e: e.reduce_sum(w2f[:], w1f[:], axis=AX.X))
    kp_v = vop(lambda e: e.tensor_single_scalar(
        keepf[:], w2f[:], float(CAP), ALU.is_lt))

    wait("gpsimd", "s_vec", kp_v)
    b_v = op("gpsimd", "s_st", lambda e: e.dma_start(out=scr3[0], in_=w2f[:]), 16)
    b_v = op("gpsimd", "s_st", lambda e: e.dma_start(out=scr3[1], in_=etf[:]), 16)
    b_v = op("gpsimd", "s_st", lambda e: e.dma_start(out=scr3[2], in_=keepf[:]), 16)
    wait("gpsimd", "s_ld", ld_v)

    def own_loads(e):
        with e.register("cid") as cr:
            e.reg_load(cr, cids[0:1, 0:1])
            cv = e.compute_val(cr)
            e.wait_ge(sems["s_st"], b_v)
            e.dma_start(out=p_own[:], in_=scr3[0][:, bass.ts(cv, NT)]).then_inc(
                sems["s_gp"], 16)
            e.dma_start(out=e_own[:], in_=scr3[1][:, bass.ts(cv, NT)]).then_inc(
                sems["s_gp"], 16)
            e.dma_start(out=k_own[:], in_=scr3[2][:, bass.ts(cv, NT)]).then_inc(
                sems["s_gp"], 16)
    on("gpsimd", own_loads)
    cnt["s_gp"] += 48
    own_v = cnt["s_gp"]

    wait("vector", "s_gp", own_v)
    vop(lambda e: e.tensor_tensor(gate[:], gvrs[:], k_own[:], ALU.mult))
    vop(lambda e: e.tensor_scalar_mul(sl1[:], e_own[:], float(CAP)))
    vop(lambda e: e.tensor_tensor(sl1[:], sl1[:], p_own[:], ALU.add))
    vop(lambda e: e.tensor_scalar_mul(sl2[:], k_own[:], -BIGF))
    vop(lambda e: e.tensor_scalar_add(sl2[:], sl2[:], BIGF))
    vop(lambda e: e.tensor_tensor(sl1[:], sl1[:], sl2[:], ALU.add))
    sf_v = vop(lambda e: e.tensor_copy(slots[:], sl1[:]))

    wait("gpsimd", "s_vec", sf_v)
    op("gpsimd", "s_st", lambda e: e.dma_start(
        out=slotg_out, in_=slots[:].bitcast(I32)), 16)
    op("gpsimd", "s_st", lambda e: e.dma_start(out=gate_out, in_=gate[:]), 16)
    fill_v = op("gpsimd", "s_st", lambda e: e.dma_start(
        out=invp_out.rearrange("(p f) o -> p (f o)", p=128), in_=sentt[:]), 16)
    wait("gpsimd", "s_st", fill_v)
    for j in range(NT):
        op("gpsimd", "s_st", lambda e, j_=j: e.indirect_dma_start(
            invp_out, bass.IndirectOffsetOnAxis(slots[:, j_:j_ + 1], 0),
            iotg[:, j_:j_ + 1], None,
            bounds_check=TT - 1, oob_is_err=False), 16)
    wait("gpsimd", "s_st", cnt["s_st"])

    block = ctx.enter_context(nc.Block())

    @block.sync
    def _(e):
        for f in OPS["sync"]:
            f(e)

    @block.vector
    def _(e):
        for f in OPS["vector"]:
            f(e)

    @block.scalar
    def _(e):
        for f in OPS["scalar"]:
            f(e)

    @block.tensor
    def _(e):
        for f in OPS["tensor"]:
            f(e)

    @block.gpsimd
    def _(e):
        for f in OPS["gpsimd"]:
            f(e)

    ctx.close()
    return nc


def l2a_inputs(full, l1res):
    tri = (np.arange(128)[:, None] < np.arange(128)[None, :]).astype(np.float32)
    eidx = np.tile(np.arange(E, dtype=np.float32), (128, 1))
    ebig = eidx - 1.0e6
    ones = np.ones((128, 1), np.float32)
    u_all = np.concatenate([l1res[c]["u_out"] for c in range(NCORES)], axis=1)
    maps = []
    for c in range(NCORES):
        iotg = (np.arange(NT)[None, :] * 128 + np.arange(128)[:, None]
                + c * T).astype(np.int32)
        maps.append(dict(
            u_all=u_all, gvr_in=l1res[c]["gvr_out"], tri_in=tri, eidx_in=eidx,
            ebig_in=ebig, ones_in=ones, onesr_in=np.ones((1, 128), np.float32), iotg_in=iotg,
            cid_in=np.array([[c]], np.int32)))
    return maps


def build_l3():
    nc = bass.Bass("TRN2", target_bir_lowering=False, debug=False,
                   num_devices=NCORES)
    xn_all = nc.dram_tensor("xn_all", [TT, H], BF16, kind="ExternalInput").ap()
    invp = nc.dram_tensor("invp", [NCORES, CAP], I32, kind="ExternalInput").ap()
    w1_in = nc.dram_tensor("w1_in", [H, HF], F32, kind="ExternalInput").ap()
    b1_in = nc.dram_tensor("b1_in", [1, HF], F32, kind="ExternalInput").ap()
    w2_in = nc.dram_tensor("w2_in", [HF, H], F32, kind="ExternalInput").ap()
    b2_in = nc.dram_tensor("b2_in", [1, H], F32, kind="ExternalInput").ap()
    eo_out = nc.dram_tensor("eo_out", [CAP, H], BF16, kind="ExternalOutput").ap()
    invm_d = nc.dram_tensor("invm_d", [CAP, 1], I32, kind="Internal").ap()

    ctx = contextlib.ExitStack()
    sbt = lambda n, s, d=F32: ctx.enter_context(nc.sbuf_tensor(n, s, d))
    pst = lambda n, s, d=F32: ctx.enter_context(nc.psum_tensor(n, s, d))
    sem = lambda n: ctx.enter_context(nc.semaphore(n))

    NCH3 = CAP // 128
    NFB = HF // 128

    w1sb = sbt("w1sb", [128, NHB, HF], BF16)
    w2sb = sbt("w2sb", [128, NFB, H], BF16)
    b1sb = sbt("b1sb", [1, HF], BF16)
    b2sb = sbt("b2sb", [1, H], BF16)
    onesb = sbt("onesb", [1, 128], BF16)
    ips = sbt("ips", [128, NCORES, NT], I32)
    invm = sbt("invm", [128, NT], I32)
    idxc = sbt("idxc", [128, NCH3], U32)
    disp = [sbt(f"disp{k}", [128, H], BF16) for k in range(2)]
    dispT = [sbt(f"dispT{k}", [128, NHB, 128], BF16) for k in range(2)]
    h1 = [sbt(f"h1{k}", [128, HF], BF16) for k in range(2)]
    h1T = [sbt(f"h1T{k}", [128, NFB, 128], BF16) for k in range(2)]
    eos = [sbt(f"eos{k}", [128, H], BF16) for k in range(2)]

    ps_h = [pst(f"ps_h{k}", [128, 512]) for k in range(2)]
    ps_e = [pst(f"ps_e{k}", [128, 512]) for k in range(2)]

    sems = {n: sem(n) for n in ["s_w", "s_ld", "s_vec", "s_scl", "s_xb",
                                "s_mm", "s_st", "s_gp"]}
    cnt = {n: 0 for n in sems}
    OPS = {k: [] for k in ["sync", "vector", "scalar", "tensor", "gpsimd"]}

    def on(eng, fn):
        OPS[eng].append(fn)

    def inc(s, k=1):
        cnt[s] += k
        return cnt[s]

    def wait(eng, s, v):
        if v > 0:
            on(eng, lambda e, ss=sems[s], vv=v: e.wait_ge(ss, vv))

    def op(eng, s, emit, k=1):
        v = inc(s, k)
        on(eng, lambda e, ss=sems[s], kk=k, em=emit: em(e).then_inc(ss, kk))
        return v

    op("gpsimd", "s_w", lambda e: e.dma_start(
        out=w1sb[:], in_=w1_in.rearrange("(mb p) hf -> p mb hf", p=128)), 16)
    op("gpsimd", "s_w", lambda e: e.dma_start(
        out=w2sb[:], in_=w2_in.rearrange("(fb p) m -> p fb m", p=128)), 16)
    op("gpsimd", "s_w", lambda e: e.dma_start(out=b1sb[:], in_=b1_in), 16)
    w_v = op("gpsimd", "s_w", lambda e: e.dma_start(out=b2sb[:], in_=b2_in), 16)

    ld_v = op("sync", "s_ld", lambda e: e.dma_start(
        out=ips[:], in_=invp.rearrange("c (p f) -> p c f", p=128)), 16)
    wait("vector", "s_ld", ld_v)
    mn_v = op("vector", "s_vec", lambda e: e.tensor_copy(invm[:], ips[:, 0, :]))
    for c in range(1, NCORES):
        wait("vector", "s_vec", mn_v)
        mn_v = op("vector", "s_vec", lambda e, c_=c: e.tensor_tensor(
            invm[:], invm[:], ips[:, c_, :], ALU.min))
    on_v = op("vector", "s_vec", lambda e: e.memset(onesb[:], 1.0))
    wait("gpsimd", "s_vec", mn_v)
    bb_v = op("gpsimd", "s_st", lambda e: e.dma_start(
        out=invm_d.rearrange("(p f) o -> p (f o)", p=128), in_=invm[:]), 16)
    wait("sync", "s_st", bb_v)
    ic_v = 0
    for j in range(NCH3):
        ic_v = op("sync", "s_ld", lambda e, j_=j: e.dma_start(
            out=idxc[:, j_:j_ + 1], in_=invm_d[j_ * 128:(j_ + 1) * 128, :].bitcast(U32)), 16)

    gather_v = {}
    xbar1_v = {}
    xbar2_v = {}
    w1mm_done = {}
    w2mm_done = {}
    relu_done = {}
    eo_ev = {}
    st_v = {}
    last_ps_h = {0: 0, 1: 0}
    last_ps_e = {0: 0, 1: 0}
    wait("gpsimd", "s_ld", ic_v)
    for j in range(NCH3):
        k = j % 2
        if j >= 2:
            wait("gpsimd", "s_xb", xbar1_v[j - 2])
        gather_v[j] = op("gpsimd", "s_gp", lambda e, j_=j, k_=k: e.indirect_dma_start(
            disp[k_][:], None, xn_all,
            bass.IndirectOffsetOnAxis(idxc[:, j_:j_ + 1], 0),
            bounds_check=TT - 1, oob_is_err=False), 16)
        wait("scalar", "s_gp", gather_v[j])
        if j >= 2:
            wait("scalar", "s_mm", w1mm_done[j - 2])
        xbar1_v[j] = op("scalar", "s_xb", lambda e, k_=k: e.dma_start(
            out=dispT[k_][:], in_=disp[k_][:], transpose=True), 16)
        if j == 0:
            wait("tensor", "s_w", w_v)
            wait("tensor", "s_vec", on_v)
        wait("tensor", "s_xb", xbar1_v[j])
        for hfc in range(8):
            pp = hfc % 2
            wait("tensor", "s_scl", last_ps_h[pp])
            for mb in range(NHB):
                op("tensor", "s_mm", lambda e, k_=k, m_=mb, h_=hfc, p_=pp:
                   e.matmul(ps_h[p_][:], dispT[k_][:, m_, :],
                            w1sb[:, m_, h_ * 512:(h_ + 1) * 512],
                            start=(m_ == 0), stop=False))
            mm_v = op("tensor", "s_mm", lambda e, h_=hfc, p_=pp: e.matmul(
                ps_h[p_][:], onesb[:], b1sb[:, h_ * 512:(h_ + 1) * 512],
                start=False, stop=True))
            if hfc == 0 and j >= 2:
                wait("scalar", "s_xb", xbar2_v[j - 2])
            wait("scalar", "s_mm", mm_v)
            last_ps_h[pp] = op("scalar", "s_scl",
                               lambda e, k_=k, h_=hfc, p_=pp: e.activation(
                                   h1[k_][:, h_ * 512:(h_ + 1) * 512],
                                   ps_h[p_][:], AF.Relu))
        w1mm_done[j] = cnt["s_mm"]
        relu_done[j] = cnt["s_scl"]
        wait("scalar", "s_scl", relu_done[j])
        if j >= 2:
            wait("scalar", "s_mm", w2mm_done[j - 2])
        xbar2_v[j] = op("scalar", "s_xb", lambda e, k_=k: e.dma_start(
            out=h1T[k_][:], in_=h1[k_][:], transpose=True), 16)
        wait("tensor", "s_xb", xbar2_v[j])
        for mh in range(2):
            pp = mh
            wait("tensor", "s_scl", last_ps_e[pp])
            for fb in range(NFB):
                op("tensor", "s_mm", lambda e, k_=k, f_=fb, m_=mh, p_=pp:
                   e.matmul(ps_e[p_][:], h1T[k_][:, f_, :],
                            w2sb[:, f_, m_ * 512:(m_ + 1) * 512],
                            start=(f_ == 0), stop=False))
            mm_v = op("tensor", "s_mm", lambda e, m_=mh, p_=pp: e.matmul(
                ps_e[p_][:], onesb[:], b2sb[:, m_ * 512:(m_ + 1) * 512],
                start=False, stop=True))
            wait("scalar", "s_mm", mm_v)
            if j >= 2 and mh == 0:
                wait("scalar", "s_st", st_v[j - 2])
            last_ps_e[pp] = op("scalar", "s_scl",
                               lambda e, k_=k, m_=mh, p_=pp: e.activation(
                                   eos[k_][:, m_ * 512:(m_ + 1) * 512],
                                   ps_e[p_][:], AF.Identity))
            eo_ev[(j, mh)] = last_ps_e[pp]
        w2mm_done[j] = cnt["s_mm"]
        wait("gpsimd", "s_scl", eo_ev[(j, 1)])
        st_v[j] = op("gpsimd", "s_st", lambda e, j_=j, k_=k: e.dma_start(
            out=eo_out[j_ * 128:(j_ + 1) * 128, :], in_=eos[k_][:]), 16)
    wait("gpsimd", "s_st", cnt["s_st"])

    block = ctx.enter_context(nc.Block())

    @block.sync
    def _(e):
        for f in OPS["sync"]:
            f(e)

    @block.vector
    def _(e):
        for f in OPS["vector"]:
            f(e)

    @block.scalar
    def _(e):
        for f in OPS["scalar"]:
            f(e)

    @block.tensor
    def _(e):
        for f in OPS["tensor"]:
            f(e)

    @block.gpsimd
    def _(e):
        for f in OPS["gpsimd"]:
            f(e)

    ctx.close()
    return nc


def l3_inputs(full, l1res, l2res):
    xn_all = np.concatenate([l1res[c]["xn_out"] for c in range(NCORES)], axis=0)
    invps = np.stack([l2res[c]["invp_out"][:, 0] for c in range(NCORES)], axis=0)
    w1 = np.asarray(full["L0_ffn_w1"], np.float32)
    b1 = np.asarray(full["L0_ffn_b1"], np.float32)
    w2 = np.asarray(full["L0_ffn_w2"], np.float32)
    b2 = np.asarray(full["L0_ffn_b2"], np.float32)
    maps = []
    for e in range(NCORES):
        maps.append(dict(
            xn_all=xn_all,
            invp=np.ascontiguousarray(invps[:, e * CAP:(e + 1) * CAP]),
            w1_in=np.ascontiguousarray(w1[e]), b1_in=b1[e][None, :],
            w2_in=np.ascontiguousarray(w2[e]), b2_in=b2[e][None, :]))
    return maps


def build_l4():
    nc = bass.Bass("TRN2", target_bir_lowering=False, debug=False,
                   num_devices=NCORES)
    x_in = nc.dram_tensor("x_in", [T, H], F32, kind="ExternalInput").ap()
    attn_in = nc.dram_tensor("attn_in", [T, H], BF16, kind="ExternalInput").ap()
    eo_all = nc.dram_tensor("eo_all", [TT, H], BF16, kind="ExternalInput").ap()
    slotg_in = nc.dram_tensor("slotg_in", [128, NT], I32, kind="ExternalInput").ap()
    gate_in = nc.dram_tensor("gate_in", [128, NT], F32, kind="ExternalInput").ap()
    out = nc.dram_tensor("out", [T, H], F32, kind="ExternalOutput").ap()

    ctx = contextlib.ExitStack()
    sbt = lambda n, s, d=F32: ctx.enter_context(nc.sbuf_tensor(n, s, d))
    sem = lambda n: ctx.enter_context(nc.semaphore(n))

    xts = [sbt(f"xts{k}", [128, H]) for k in range(2)]
    ats = [sbt(f"ats{k}", [128, H], BF16) for k in range(2)]
    eog = [sbt(f"eog{k}", [128, H], BF16) for k in range(2)]
    t1s = [sbt(f"t1s{k}", [128, H]) for k in range(2)]
    outs = [sbt(f"outs{k}", [128, H]) for k in range(2)]
    slg = sbt("slg", [128, NT], U32)
    gts = sbt("gts", [128, NT])

    sems = {n: sem(n) for n in ["s_ld", "s_gp", "s_vec", "s_st"]}
    cnt = {n: 0 for n in sems}
    OPS = {k: [] for k in ["sync", "vector", "scalar", "tensor", "gpsimd"]}

    def on(eng, fn):
        OPS[eng].append(fn)

    def inc(s, k=1):
        cnt[s] += k
        return cnt[s]

    def wait(eng, s, v):
        if v > 0:
            on(eng, lambda e, ss=sems[s], vv=v: e.wait_ge(ss, vv))

    def op(eng, s, emit, k=1):
        v = inc(s, k)
        on(eng, lambda e, ss=sems[s], kk=k, em=emit: em(e).then_inc(ss, kk))
        return v

    pr_v = op("sync", "s_ld", lambda e: e.dma_start(
        out=slg[:], in_=slotg_in.bitcast(U32)), 16)
    pr_v = op("sync", "s_ld", lambda e: e.dma_start(out=gts[:], in_=gate_in), 16)
    z_v = op("vector", "s_vec", lambda e: e.memset(eog[0][:], 0.0))
    z_v = op("vector", "s_vec", lambda e: e.memset(eog[1][:], 0.0))
    wait("gpsimd", "s_vec", z_v)
    wait("gpsimd", "s_ld", pr_v)

    ld_v = {}
    gat_v = {}
    cmb_v = {}
    stt_v = {}
    for g in range(NT):
        k = g % 2
        if g >= 2:
            wait("sync", "s_vec", cmb_v[g - 2])
        ld_v[g] = op("sync", "s_ld", lambda e, g_=g, k_=k: e.dma_start(
            out=xts[k_][:], in_=x_in[g_ * 128:(g_ + 1) * 128, :]), 16)
        ld_v[g] = op("sync", "s_ld", lambda e, g_=g, k_=k: e.dma_start(
            out=ats[k_][:], in_=attn_in[g_ * 128:(g_ + 1) * 128, :]), 16)
        if g >= 2:
            wait("gpsimd", "s_vec", cmb_v[g - 2])
        gat_v[g] = op("gpsimd", "s_gp", lambda e, g_=g, k_=k: e.indirect_dma_start(
            eog[k_][:], None, eo_all,
            bass.IndirectOffsetOnAxis(slg[:, g_:g_ + 1], 0),
            bounds_check=TT - 1, oob_is_err=False), 16)
        wait("vector", "s_ld", ld_v[g])
        wait("vector", "s_gp", gat_v[g])
        if g >= 2:
            wait("vector", "s_st", stt_v[g - 2])
        t_v = op("vector", "s_vec", lambda e, k_=k: e.tensor_add(
            t1s[k_][:], xts[k_][:], ats[k_][:]))
        wait("vector", "s_vec", t_v)
        cmb_v[g] = op("vector", "s_vec", lambda e, g_=g, k_=k: e.scalar_tensor_tensor(
            outs[k_][:], eog[k_][:], gts[:, g_:g_ + 1], t1s[k_][:],
            ALU.mult, ALU.add))
        wait("vector", "s_vec", cmb_v[g])
        wait("gpsimd", "s_vec", cmb_v[g])
        stt_v[g] = op("gpsimd", "s_st", lambda e, g_=g, k_=k: e.dma_start(
            out=out[g_ * 128:(g_ + 1) * 128, :], in_=outs[k_][:]), 16)
    wait("gpsimd", "s_st", cnt["s_st"])

    block = ctx.enter_context(nc.Block())

    @block.sync
    def _(e):
        for f in OPS["sync"]:
            f(e)

    @block.vector
    def _(e):
        for f in OPS["vector"]:
            f(e)

    @block.scalar
    def _(e):
        for f in OPS["scalar"]:
            f(e)

    @block.tensor
    def _(e):
        for f in OPS["tensor"]:
            f(e)

    @block.gpsimd
    def _(e):
        for f in OPS["gpsimd"]:
            f(e)

    ctx.close()
    return nc


def l4_inputs(full, l1res, l2res, l3res):
    x = np.ascontiguousarray(np.asarray(full["L0_attn_x"], np.float32)).reshape(TT, H)
    eo_all = np.concatenate([l3res[c]["eo_out"] for c in range(NCORES)], axis=0)
    maps = []
    for c in range(NCORES):
        maps.append(dict(
            x_in=x[c * T:(c + 1) * T],
            attn_in=l1res[c]["attn_out"],
            eo_all=eo_all,
            slotg_in=l2res[c]["slotg_out"],
            gate_in=l2res[c]["gate_out"]))
    return maps


def kernel(**inputs):
    cores = list(range(NCORES))
    r1 = run_bass_kernel_spmd(build_l1(), l1_inputs(inputs), core_ids=cores).results
    r2 = run_bass_kernel_spmd(build_l2a(), l2a_inputs(inputs, r1),
                              core_ids=cores).results
    r3 = run_bass_kernel_spmd(build_l3(), l3_inputs(inputs, r1, r2),
                              core_ids=cores).results
    r4 = run_bass_kernel_spmd(build_l4(), l4_inputs(inputs, r1, r2, r3),
                              core_ids=cores).results
    out = np.concatenate([r4[c]["out"] for c in range(NCORES)], axis=0)
    return out.reshape(B, S, H)
